# version 01
# baseline (speedup 1.0000x reference)
import numpy as np
from contextlib import ExitStack
import concourse.bass as bass
import concourse.mybir as mybir
from concourse.alu_op_type import AluOpType as OP
from concourse.bass_utils import run_bass_kernel_spmd

F32 = mybir.dt.float32
BF16 = mybir.dt.bfloat16
FP8 = mybir.dt.float8e4
AF = mybir.ActivationFunctionType
EPS = 1e-5
EPSB = 1e-4
L, DM, DI = 4096, 1024, 2048
NT, TB = DI // 128, L // 512

def _load_table(nc, set_name):
    from concourse.hw_specs import get_activation_tables
    tabs = get_activation_tables(nc.m.arch)
    idx = list(tabs).index(set_name)
    return nc.scalar.add_instruction(mybir.InstLoadActFuncSet(
        name=nc.get_next_instruction_name(), act_func_set_id=idx,
        ins=[], outs=[]))

def build_l1(phases=6):
    nc = bass.Bass("TRN2", target_bir_lowering=False)
    xb    = nc.dram_tensor("xb", [DM, L], F32, kind="ExternalInput").ap()
    normw = nc.dram_tensor("normw", [DM], F32, kind="ExternalInput").ap()
    normb = nc.dram_tensor("normb", [DM], F32, kind="ExternalInput").ap()
    inwT  = nc.dram_tensor("inwT", [DM, 2*DI], F32, kind="ExternalInput").ap()
    convw = nc.dram_tensor("convw", [DI, 4], F32, kind="ExternalInput").ap()
    convb = nc.dram_tensor("convb", [DI], F32, kind="ExternalInput").ap()
    xwT   = nc.dram_tensor("xwT", [DI, 96], F32, kind="ExternalInput").ap()
    dtpwT = nc.dram_tensor("dtpwT", [64, DI], F32, kind="ExternalInput").ap()
    dtpb  = nc.dram_tensor("dtpb", [DI], F32, kind="ExternalInput").ap()
    alog  = nc.dram_tensor("alog", [DI, 16], F32, kind="ExternalInput").ap()
    dvec  = nc.dram_tensor("dvec", [DI], F32, kind="ExternalInput").ap()
    moutwT = nc.dram_tensor("moutwT", [DI, DM], F32, kind="ExternalInput").ap()
    twl   = nc.dram_tensor("twl", [2], F32, kind="ExternalInput").ap()
    part  = nc.dram_tensor("part", [DM, L], F32, kind="ExternalOutput").ap()
    dbgo  = nc.dram_tensor("dbgo", [128, 2*L], F32, kind="ExternalOutput").ap()
    u_sp  = nc.dram_tensor("u_sp", [DI, L], BF16, kind="Internal").ap()
    g_sp  = nc.dram_tensor("g_sp", [DI, L], BF16, kind="Internal").ap()
    dt_sp = nc.dram_tensor("dt_sp", [DI, L], BF16, kind="Internal").ap()
    gated_sp = nc.dram_tensor("gated_sp", [DI, L], BF16, kind="Internal").ap()
    bc_sp = nc.dram_tensor("bc_sp", [32, L], F32, kind="Internal").ap()
    row_sp = nc.dram_tensor("row_sp", [4, L], F32, kind="Internal").ap()

    ctx = ExitStack()
    ARENA = 51200
    arena = ctx.enter_context(nc.sbuf_tensor("arena", [128, ARENA], F32))
    off = [0]
    def af32(n):
        ap = arena[:, off[0]:off[0]+n]; off[0] += n
        return ap
    def abf16(n):
        m = (n+1)//2
        return af32(m).bitcast(BF16)
    s_nw = af32(8); s_nb = af32(8); s_cw = af32(64); s_cb = af32(16)
    s_dtpb = af32(16); s_Dv = af32(16)
    s_eA = af32(256); s_negA = af32(256)
    s_wcol = af32(1); s_tw = af32(8)
    s_onesb = abf16(2)
    s_stat = af32(64); s_m1 = af32(32); s_v1 = af32(32); s_r1 = af32(32)
    s_Rc = af32(32); s_mRc = af32(32)
    s_consts = af32(4)
    PERM_END = off[0]
    XREG = off[0]
    s_xt = [af32(L) for _ in range(8)]
    s_hn = [x.bitcast(BF16) for x in s_xt]
    XREG_END = off[0]
    def at(pos):
        off[0] = pos
    s_xbf = [abf16(L) for _ in range(2)]
    s_sq  = [abf16(L) for _ in range(2)]
    s_Rrep = abf16(L); s_mRrep = abf16(L); s_t1 = abf16(L)
    B1_END = off[0]
    at(XREG_END)
    s_xw = abf16(96*NT)
    s_dtpw = abf16(128*NT)
    s_xpre = [af32(516) for _ in range(2)]
    s_cvt  = [af32(512) for _ in range(2)]
    s_utb  = abf16(NT*512)
    s_halo = af32(3*NT)
    s_xdbl = af32(L)
    s_xdblb = abf16(L)
    s_ee   = [af32(512) for _ in range(2)]
    s_dtt  = [abf16(512) for _ in range(2)]
    C_END = off[0]
    at(XREG)
    s_dt4  = [abf16(L) for _ in range(4)]
    s_ug4  = [abf16(L) for _ in range(4)]
    s_dtu4 = [abf16(L) for _ in range(4)]
    s_yac4 = [abf16(L) for _ in range(4)]
    s_Brep = [abf16(L) for _ in range(2)]
    s_Crep = [abf16(L) for _ in range(2)]
    s_dA   = [abf16(L) for _ in range(2)]
    s_xin  = abf16(L)
    s_h    = abf16(L)
    s_T4 = s_xin
    E_END = off[0]
    at(XREG)
    s_moutw = abf16(16*DM)
    s_grt  = [abf16(NT*512) for _ in range(2)]
    s_po   = [af32(512) for _ in range(2)]
    F_END = off[0]
    assert max(B1_END, C_END, E_END, F_END) <= ARENA, (PERM_END, B1_END, C_END, E_END, F_END)

    def ps(name, shape):
        return ctx.enter_context(nc.psum_tensor(name, shape, F32))
    p_mmx = [ps(f"p_mmx{i}", [128, 512]) for i in range(2)]
    p_mmz = [ps(f"p_mmz{i}", [128, 512]) for i in range(2)]
    p_xd  = ps("p_xd", [96, 512])
    p_stat = ps("p_stat", [128, 64])
    p_dtp = [ps(f"p_dtp{i}", [128, 512]) for i in range(2)]

    sems = {}
    def S(name):
        if name not in sems:
            sems[name] = ctx.enter_context(nc.semaphore(name))
        return sems[name]
    for n in (["ld","gld","bc","ldr","ldb","ldw","bw","v","se","su","sg","vc","vx","t","tx","t2","s3",
               "sa","vs","vg","mo","so","mis","spd0","spd1","spz0","spz1","rbc0","rbc1","spo0","spo1",
               "rgr0","rgr1"]
              + [f"spu{t}" for t in range(16)] + [f"rlu{d}" for d in range(4)] + [f"rld{d}" for d in range(4)]
              + [f"spg{d}" for d in range(4)]):
        S(n)

    def sp_u(tb, t): return tb*32 + t + 1
    def sp_z(tb, t): return tb*32 + 16 + t + 1
    SP_C = TB*32
    SP_D = SP_C + NT*TB
    def VS(sg, n, d): return (sg*16 + n)*4 + d + 1

    ctx.enter_context(nc.allow_non_contiguous_dma(reason="small strided param loads"))
    with ctx, nc.Block() as block:
        @block.sync
        def _(s):
            for t in range(8):
                s.dma_start(out=s_xt[t], in_=xb[t*128:(t+1)*128, :]).then_inc(S("ld"), 16)
            s.dma_start(out=s_nw, in_=normw.rearrange("(a b) -> b a", b=128)).then_inc(S("ld"), 16)
            s.dma_start(out=s_nb, in_=normb.rearrange("(a b) -> b a", b=128)).then_inc(S("ld"), 16)
            s.dma_start(out=s_cb, in_=convb.rearrange("(a b) -> b a", b=128)).then_inc(S("ld"), 16)
            s.dma_start(out=s_dtpb, in_=dtpb.rearrange("(a b) -> b a", b=128)).then_inc(S("ld"), 16)
            s.dma_start(out=s_Dv, in_=dvec.rearrange("(a b) -> b a", b=128)).then_inc(S("ld"), 16)
            for kk in range(4):
                s.dma_start(out=s_cw[:, kk*16:(kk+1)*16], in_=convw[:, kk:kk+1].rearrange("(a b) o -> b (a o)", b=128)).then_inc(S("ld"), 16)
            s.dma_start(out=s_tw[0:1, 0:2], in_=twl.unsqueeze(0)).then_inc(S("ld"), 16)
            s.dma_start(out=s_eA.rearrange("p (t n) -> p t n", n=16), in_=alog.rearrange("(t p) n -> p t n", p=128)).then_inc(S("ld"), 16)
        LD_A = 19*16
        @block.gpsimd
        def _(g):
            g.wait_ge(S("t"), 8)
            g.dma_start(out=s_xw.rearrange("p (t c) -> p t c", c=96), in_=xwT.rearrange("(t p) c -> p t c", p=128)).then_inc(S("gld"), 16)
            g.dma_start(out=s_dtpw[0:64, :], in_=dtpwT).then_inc(S("gld"), 16)
        GLD_A = 32

        @block.vector
        def _(v):
            v.memset(s_onesb, 1.0)
            v.memset(s_consts[:, 0:1], EPS)
            nc.const_aps.register(F32, EPS, s_consts[:, 0:1])
            v.memset(s_consts[:, 1:2], 1.0)
            nc.const_aps.register(F32, 1.0, s_consts[:, 1:2])
            v.memset(s_consts[:, 2:3], -1.0)
            nc.const_aps.register(F32, -1.0, s_consts[:, 2:3])
            v.wait_ge(S("ld"), LD_A)
            for t in range(8):
                if t >= 2:
                    v.wait_ge(S("t"), t - 1)
                v.tensor_copy(out=s_xbf[t % 2], in_=s_xt[t])
                v.drain()
                v.tensor_tensor(out=s_sq[t % 2], in0=s_xbf[t % 2], in1=s_xbf[t % 2], op=OP.mult).then_inc(S("v"), 1)
        @block.tensor
        def _(te):
            for t in range(8):
                te.wait_ge(S("v"), t + 1)
                last = None
                for tt in range(32):
                    te.matmul(p_stat[:, 2*tt:2*tt+1], s_xbf[t % 2][:, tt*128:(tt+1)*128],
                              s_onesb[0:128, 0:1], start=(t == 0 and tt == 0), stop=False)
                    last = te.matmul(p_stat[:, 2*tt+1:2*tt+2], s_sq[t % 2][:, tt*128:(tt+1)*128],
                                     s_onesb[0:128, 0:1], start=False, stop=(t == 7 and tt == 31))
                last.then_inc(S("t"), 1)
        @block.vector
        def _(v):
            v.wait_ge(S("t"), 8)
            v.tensor_copy(out=s_stat, in_=p_stat[:, :])
            v.drain()
            v.tensor_scalar(out=s_m1, in0=s_stat[:, 0:64:2], scalar1=1.0/DM, scalar2=None, op0=OP.mult)
            v.tensor_scalar(out=s_v1, in0=s_stat[:, 1:64:2], scalar1=1.0/DM, scalar2=None, op0=OP.mult)
            v.drain()
            v.tensor_tensor(out=s_r1, in0=s_m1, in1=s_m1, op=OP.mult)
            v.drain()
            v.tensor_tensor(out=s_v1, in0=s_v1, in1=s_r1, op=OP.subtract).then_inc(S("v"), 1)
        @block.scalar
        def _(sc):
            sc.preload_act_table(AF.Sqrt)
            sc.wait_ge(S("v"), 9)
            sc.activation(out=s_r1, in_=s_v1, func=AF.Sqrt, bias=EPS).then_inc(S("s3"), 1)
        @block.vector
        def _(v):
            v.wait_ge(S("s3"), 1)
            v.reciprocal(out=s_r1, in_=s_r1)
            v.tensor_scalar(out=s_Rc, in0=s_v1, scalar1=EPS, scalar2=None, op0=OP.add)
            v.drain()
            v.reciprocal(out=s_Rc, in_=s_Rc)
            v.drain()
            v.tensor_tensor(out=s_Rc, in0=s_v1, in1=s_Rc, op=OP.mult).then_inc(S("v"), 1)
        @block.scalar
        def _(sc):
            sc.wait_ge(S("v"), 10)
            sc.activation(out=s_Rc, in_=s_Rc, func=AF.Sqrt, bias=EPS).then_inc(S("s3"), 1)
        @block.vector
        def _(v):
            v.wait_ge(S("s3"), 2)
            v.reciprocal(out=s_Rc, in_=s_Rc)
            v.drain()
            v.tensor_tensor(out=s_Rc, in0=s_Rc, in1=s_r1, op=OP.mult)
            v.drain()
            v.tensor_tensor(out=s_mRc, in0=s_m1, in1=s_Rc, op=OP.mult).then_inc(S("v"), 1)
        @block.sync
        def _(s):
            s.wait_ge(S("v"), 11)
            s.dma_start(out=row_sp[0:1, :].rearrange("o (a b) -> (o b) a", b=128), in_=s_Rc).then_inc(S("ldr"), 16)
            s.dma_start(out=row_sp[1:2, :].rearrange("o (a b) -> (o b) a", b=128), in_=s_mRc).then_inc(S("ldr"), 16)
        @block.gpsimd
        def _(g):
            g.wait_ge(S("ldr"), 32)
            g.dma_start(out=s_Rrep, in_=row_sp[0:1, :].broadcast_to((128, L))).then_inc(S("bc"), 16)
            g.dma_start(out=s_mRrep, in_=row_sp[1:2, :].broadcast_to((128, L))).then_inc(S("bc"), 16)
            for k in range(8):
                g.wait_ge(S("v"), 12 + k)
                g.dma_start(out=s_hn[k][:, L:2*L], in_=inwT[k*128:(k+1)*128, :]).then_inc(S("gld"), 16)
        s_inw = [h[:, L:2*L] for h in s_hn]
        GLD_C = GLD_A + 8*16
        @block.vector
        def _(v):
            v.wait_ge(S("bc"), 32)
            for t in range(8):
                v.drain()
                v.tensor_tensor(out=s_t1, in0=s_xt[t], in1=s_Rrep, op=OP.mult)
                v.drain()
                v.tensor_tensor(out=s_t1, in0=s_t1, in1=s_mRrep, op=OP.subtract)
                v.drain()
                v.tensor_scalar(out=s_hn[t][:, 0:L], in0=s_t1, scalar1=s_nw[:, t:t+1], scalar2=s_nb[:, t:t+1],
                                op0=OP.mult, op1=OP.add).then_inc(S("v"), 1)
        HN_V = 19
        if phases <= 1:
            @block.gpsimd
            def _(g):
                g.wait_ge(S("v"), HN_V)
                g.dma_start(out=dbgo[:, 0:L], in_=s_hn[0][:, 0:L]).then_inc(S("so"), 16)
                g.dma_start(out=dbgo[:, L:2*L], in_=s_hn[7][:, 0:L]).then_inc(S("so"), 16)
                g.wait_ge(S("so"), 32)
            ctx.close()
            return nc

        @block.tensor
        def _(te):
            te.wait_ge(S("gld"), GLD_C)
            te.wait_ge(S("v"), HN_V)
            for tb in range(TB):
                for t in range(NT):
                    i = tb*NT + t
                    if i >= 2:
                        te.wait_ge(S("se"), i - 1)
                    mm = None
                    for k in range(8):
                        mm = te.matmul(p_mmx[i % 2][:, :], s_inw[k][:, t*128:(t+1)*128],
                                       s_hn[k][:, tb*512:(tb+1)*512], start=(k == 0), stop=(k == 7))
                    mm.then_inc(S("t"), 1)
                for t in range(NT):
                    i = tb*NT + t
                    if i >= 2:
                        te.wait_ge(S("sg"), i - 1)
                    mm = None
                    for k in range(8):
                        mm = te.matmul(p_mmz[i % 2][:, :], s_inw[k][:, DI + t*128:DI + (t+1)*128],
                                       s_hn[k][:, tb*512:(tb+1)*512], start=(k == 0), stop=(k == 7))
                    mm.then_inc(S("t2"), 1)
                if tb >= 1:
                    te.wait_ge(S("vx"), tb)
                mm = None
                for t in range(NT):
                    te.wait_ge(S("su"), tb*NT + t + 1)
                    mm = te.matmul(p_xd[:, :], s_xw[:, t*96:(t+1)*96],
                                   s_utb[:, t*512:(t+1)*512], start=(t == 0), stop=(t == NT-1))
                mm.then_inc(S("tx"), 1)
        T_B = 8
        @block.scalar
        def _(sc):
            sc.preload_act_table(AF.Silu)
            for tb in range(TB):
                for t in range(NT):
                    i = tb*NT + t
                    sc.wait_ge(S("t"), T_B + i + 1)
                    if i >= 2:
                        sc.wait_ge(S("vc"), i - 1)
                    sc.copy(out=s_xpre[i % 2][:, 3:515], in_=p_mmx[i % 2][:, :]).then_inc(S("se"), 1)
                    sc.wait_ge(S("vc"), i + 1)
                    if tb >= 1:
                        sc.wait_ge(S(f"spu{t}"), tb*16)
                        sc.wait_ge(S("tx"), tb)
                    sc.activation(out=s_utb[:, t*512:(t+1)*512], in_=s_cvt[i % 2],
                                  func=AF.Silu, bias=s_cb[:, t:t+1]).then_inc(S("su"), 1)
                for t in range(NT):
                    i = tb*NT + t
                    sc.wait_ge(S("t2"), i + 1)
                    if i >= 2:
                        sc.wait_ge(S(f"spz{i % 2}"), (i // 2)*16)
                    sc.activation(out=s_dtt[i % 2], in_=p_mmz[i % 2][:, :], func=AF.Silu).then_inc(S("sg"), 1)
        @block.vector
        def _(v):
            for tb in range(TB):
                for t in range(NT):
                    i = tb*NT + t
                    v.wait_ge(S("se"), i + 1)
                    if i >= 2:
                        v.wait_ge(S("su"), i - 1)
                    xp = s_xpre[i % 2]
                    if tb == 0:
                        v.memset(xp[:, 0:3], 0.0)
                    else:
                        v.tensor_copy(out=xp[:, 0:3], in_=s_halo[:, t*3:(t+1)*3])
                    v.drain()
                    v.tensor_scalar(out=s_cvt[i % 2], in0=xp[:, 0:512], scalar1=s_cw[:, 0*16+t:0*16+t+1],
                                    scalar2=None, op0=OP.mult)
                    for k in (1, 2, 3):
                        v.drain()
                        v.scalar_tensor_tensor(out=s_cvt[i % 2], in0=xp[:, k:512+k],
                                               scalar=s_cw[:, k*16+t:k*16+t+1], in1=s_cvt[i % 2],
                                               op0=OP.mult, op1=OP.add)
                    v.tensor_copy(out=s_halo[:, t*3:(t+1)*3], in_=xp[:, 512:515]).then_inc(S("vc"), 1)
                v.wait_ge(S("tx"), tb + 1)
                v.tensor_copy(out=s_xdbl[0:96, tb*512:(tb+1)*512], in_=p_xd[:, :]).then_inc(S("vx"), 1)
        @block.gpsimd
        def _(g):
            for tb in range(TB):
                for t in range(NT):
                    g.wait_ge(S("su"), tb*NT + t + 1)
                    g.dma_start(out=u_sp[(t*128):(t+1)*128, tb*512:(tb+1)*512],
                                in_=s_utb[:, t*512:(t+1)*512]).then_inc(S(f"spu{t}"), 16)
                for t in range(NT):
                    g.wait_ge(S("sg"), tb*NT + t + 1)
                    g.dma_start(out=g_sp[(t*128):(t+1)*128, tb*512:(tb+1)*512],
                                in_=s_dtt[(tb*NT + t) % 2]).then_inc(S(f"spz{(tb*NT + t) % 2}"), 16)
        if phases <= 2:
            @block.gpsimd
            def _(g):
                g.wait_ge(S("vx"), TB)
                g.dma_start(out=dbgo[0:96, 0:L], in_=s_xdbl[0:96, :]).then_inc(S("so"), 16)
                g.wait_ge(S("so"), 16)
            ctx.close()
            return nc

        @block.vector
        def _(v):
            v.wait_ge(S("vx"), TB)
            v.tensor_copy(out=s_xdblb[0:64, :], in_=s_xdbl[0:64, :])
            v.tensor_tensor(out=s_tw[0:1, 2:3], in0=s_tw[0:1, 0:1], in1=s_tw[0:1, 1:2], op=OP.subtract).then_inc(S("v"), 1)
        @block.sync
        def _(s):
            s.wait_ge(S("v"), 20)
            s.dma_start(out=bc_sp[:, :], in_=s_xdbl[64:96, :]).then_inc(S("ldb"), 16)
        @block.scalar
        def _(sc):
            _load_table(nc, "natural_log_exp_and_others")
            sc.wait_ge(S("v"), 20)
            sc.activation(out=s_negA, in_=s_eA, func=AF.Exp)
            sc.activation(out=s_tw[0:1, 3:4], in_=s_tw[0:1, 2:3], func=AF.Exp, scale=-1.0).then_inc(S("s3"), 1)
        @block.vector
        def _(v):
            v.wait_ge(S("s3"), 3)
            v.tensor_scalar(out=s_negA, in0=s_negA, scalar1=-1.0, scalar2=None, op0=OP.mult)
            v.tensor_scalar(out=s_tw[0:1, 3:4], in0=s_tw[0:1, 3:4], scalar1=1.0, scalar2=None, op0=OP.add)
            v.drain()
            v.reciprocal(out=s_tw[0:1, 2:3], in_=s_tw[0:1, 3:4]).then_inc(S("v"), 1)
        @block.sync
        def _(s):
            s.wait_ge(S("v"), 21)
            s.dma_start(out=row_sp[2:3, 0:1], in_=s_tw[0:1, 2:3]).then_inc(S("ldw"), 16)
        @block.gpsimd
        def _(g):
            g.wait_ge(S("ldw"), 16)
            g.dma_start(out=s_wcol, in_=row_sp[2:3, 0:1].broadcast_to((128, 1))).then_inc(S("bw"), 16)
        @block.tensor
        def _(te):
            te.wait_ge(S("v"), 21)
            for t in range(NT):
                for tb in range(TB):
                    j = t*TB + tb
                    if j >= 2:
                        te.wait_ge(S("s3"), 3 + j - 1)
                    te.matmul(p_dtp[j % 2][:, :], s_dtpw[0:64, t*128:(t+1)*128],
                              s_xdblb[0:64, tb*512:(tb+1)*512], start=True, stop=True).then_inc(S("t"), 1)
        T_D = T_B + TB*NT
        @block.scalar
        def _(sc):
            sc.wait_ge(S("spz0"), 64*16); sc.wait_ge(S("spz1"), 64*16)
            for t in range(NT):
                for tb in range(TB):
                    j = t*TB + tb
                    sc.wait_ge(S("t"), T_D + j + 1)
                    if j >= 2:
                        sc.wait_ge(S(f"spd{j % 2}"), (j // 2)*16)
                    sc.activation(out=s_ee[j % 2], in_=p_dtp[j % 2][:, :], func=AF.Exp, bias=s_dtpb[:, t:t+1])
                    sc.drain()
                    sc.activation(out=s_dtt[j % 2], in_=s_ee[j % 2], func=AF.Ln, bias=1.0).then_inc(S("s3"), 1)
        @block.gpsimd
        def _(g):
            for t in range(NT):
                for tb in range(TB):
                    j = t*TB + tb
                    g.wait_ge(S("s3"), 3 + j + 1)
                    g.dma_start(out=dt_sp[t*128:(t+1)*128, tb*512:(tb+1)*512],
                                in_=s_dtt[j % 2]).then_inc(S(f"spd{j % 2}"), 16)
        S3_D = 3 + NT*TB
        if phases <= 3:
            @block.gpsimd
            def _(g):
                g.wait_ge(S("spd0"), 64*16); g.wait_ge(S("spd1"), 64*16)
                g.dma_start(out=dbgo[:, 0:L], in_=dt_sp[0:128, :]).then_inc(S("so"), 16)
                g.wait_ge(S("so"), 16)
            ctx.close()
            return nc

        @block.gpsimd
        def _(g):
            g.wait_ge(S("spd0"), 64*16); g.wait_ge(S("spd1"), 64*16)
            g.wait_ge(S("ldb"), 16)
            g.wait_ge(S("spz0"), 64*16); g.wait_ge(S("spz1"), 64*16)
            for t in range(16):
                g.wait_ge(S(f"spu{t}"), 8*16)
            for sg in range(4):
                if sg > 0:
                    g.wait_ge(S("vg"), sg*4)
                for d in range(4):
                    td = sg*4 + d
                    g.dma_start(out=s_dt4[d], in_=dt_sp[td*128:(td+1)*128, :]).then_inc(S(f"rld{d}"), 16)
                    g.dma_start(out=s_ug4[d], in_=u_sp[td*128:(td+1)*128, :]).then_inc(S(f"rlu{d}"), 16)
                for n in range(16):
                    if n >= 2:
                        g.wait_ge(S("vs"), VS(sg, n-2, 3))
                    g.dma_start(out=s_Brep[n % 2], in_=bc_sp[n:n+1, :].broadcast_to((128, L))).then_inc(S(f"rbc{n % 2}"), 16)
                    g.dma_start(out=s_Crep[n % 2], in_=bc_sp[16+n:17+n, :].broadcast_to((128, L))).then_inc(S(f"rbc{n % 2}"), 16)
                g.wait_ge(S("v"), 21 + sg*4 + 4)
                for d in range(4):
                    td = sg*4 + d
                    g.dma_start(out=s_ug4[d], in_=g_sp[td*128:(td+1)*128, :]).then_inc(S(f"rlu{d}"), 16)
                for d in range(4):
                    g.wait_ge(S("vg"), sg*4 + d + 1)
                    g.dma_start(out=gated_sp[(sg*4+d)*128:(sg*4+d+1)*128, :],
                                in_=s_dtu4[d]).then_inc(S(f"spg{d}"), 16)
        @block.scalar
        def _(sc):
            for sg in range(4):
                for n in range(16):
                    for d in range(4):
                        k = VS(sg, n, d)
                        sc.wait_ge(S(f"rld{d}"), (sg + 1)*16)
                        if k > 2:
                            sc.wait_ge(S("vs"), k - 2)
                        td = sg*4 + d
                        sc.activation(out=s_dA[k % 2], in_=s_dt4[d], func=AF.Exp,
                                      scale=s_negA[:, td*16+n:td*16+n+1]).then_inc(S("sa"), 1)
        @block.vector
        def _(v):
            for sg in range(4):
                for d in range(4):
                    v.wait_ge(S(f"spg{d}"), sg*16)
                    v.wait_ge(S(f"rlu{d}"), sg*32 + 16)
                    v.wait_ge(S(f"rld{d}"), (sg + 1)*16)
                    v.drain()
                    v.tensor_tensor(out=s_dtu4[d], in0=s_dt4[d], in1=s_ug4[d], op=OP.mult)
                    v.tensor_scalar(out=s_yac4[d], in0=s_ug4[d], scalar1=s_Dv[:, sg*4+d:sg*4+d+1],
                                    scalar2=None, op0=OP.mult).then_inc(S("v"), 1)
                for n in range(16):
                    v.wait_ge(S(f"rbc{n % 2}"), (sg*8 + n//2 + 1)*32)
                    for d in range(4):
                        k = VS(sg, n, d)
                        v.wait_ge(S("sa"), k)
                        v.drain()
                        v.tensor_tensor(out=s_xin, in0=s_dtu4[d], in1=s_Brep[n % 2], op=OP.mult)
                        v.drain()
                        v.tensor_tensor_scan(out=s_h, data0=s_dA[k % 2], data1=s_xin,
                                             initial=0.0, op0=OP.mult, op1=OP.add)
                        v.drain()
                        v.tensor_tensor(out=s_T4, in0=s_h, in1=s_Crep[n % 2], op=OP.mult)
                        v.drain()
                        v.tensor_tensor(out=s_yac4[d], in0=s_yac4[d], in1=s_T4, op=OP.add).then_inc(S("vs"), 1)
                for d in range(4):
                    v.wait_ge(S(f"rlu{d}"), sg*32 + 32)
                    v.wait_ge(S("bw"), 16)
                    v.drain()
                    v.scalar_tensor_tensor(out=s_dtu4[d], in0=s_yac4[d], scalar=s_wcol[:, 0:1],
                                           in1=s_ug4[d], op0=OP.mult, op1=OP.mult).then_inc(S("vg"), 1)
        SP_E = SP_D + 16
        if phases <= 4:
            @block.gpsimd
            def _(g):
                for d in range(4):
                    g.wait_ge(S(f"spg{d}"), 4*16)
                g.dma_start(out=dbgo[:, 0:L], in_=gated_sp[0:128, :]).then_inc(S("so"), 16)
                g.wait_ge(S("so"), 16)
            ctx.close()
            return nc

        @block.gpsimd
        def _(g):
            for d in range(4):
                g.wait_ge(S(f"spg{d}"), 4*16)
            g.dma_start(out=s_moutw.rearrange("p (c m) -> p c m", m=DM), in_=moutwT.rearrange("(c p) m -> p c m", p=128)).then_inc(S("gld"), 16)
            for tb in range(TB):
                if tb >= 2:
                    g.wait_ge(S("mo"), (tb - 1)*8)
                for c in range(NT):
                    g.dma_start(out=s_grt[tb % 2][:, c*512:(c+1)*512],
                                in_=gated_sp[c*128:(c+1)*128, tb*512:(tb+1)*512]).then_inc(S(f"rgr{tb % 2}"), 16)
        RL_E = 4*12
        GLD_F = GLD_C + 16
        @block.tensor
        def _(te):
            te.wait_ge(S("gld"), GLD_F)
            for tb in range(TB):
                te.wait_ge(S(f"rgr{tb % 2}"), (tb//2 + 1)*NT*16)
                for mt in range(8):
                    j = tb*8 + mt
                    if j >= 2:
                        te.wait_ge(S("mis"), j - 1)
                    mm = None
                    for c in range(NT):
                        mm = te.matmul(p_mmx[j % 2][:, :], s_moutw[:, c*DM + mt*128: c*DM + (mt+1)*128],
                                       s_grt[tb % 2][:, c*512:(c+1)*512], start=(c == 0), stop=(c == NT-1))
                    mm.then_inc(S("mo"), 1)
        @block.vector
        def _(v):
            for j in range(64):
                v.wait_ge(S("mo"), j + 1)
                if j >= 2:
                    v.wait_ge(S(f"spo{j % 2}"), (j // 2)*16)
                v.tensor_copy(out=s_po[j % 2], in_=p_mmx[j % 2][:, :]).then_inc(S("mis"), 1)
        @block.sync
        def _(s):
            for tb in range(TB):
                for mt in range(8):
                    j = tb*8 + mt
                    s.wait_ge(S("mis"), j + 1)
                    s.dma_start(out=part[mt*128:(mt+1)*128, tb*512:(tb+1)*512],
                                in_=s_po[j % 2]).then_inc(S(f"spo{j % 2}"), 16)
            s.wait_ge(S("spo0"), 32*16); s.wait_ge(S("spo1"), 32*16)
    return nc


def build_l2(phases=9, dbg_off=None, sub=None):
    nc = bass.Bass("TRN2", target_bir_lowering=False)
    def din(name, shape, dt=F32):
        return nc.dram_tensor(name, shape, dt, kind="ExternalInput").ap()
    xb = din("xb", [DM, L]); pf = din("pf", [DM, L]); pso = din("pso", [DM, L])
    projT = din("projT", [DM, DM]); projb = din("projb", [DM])
    fker = din("fker", [DM, DM], BF16); fsin = din("fsin", [1]); fsk = din("fsk", [1])
    fbias = din("fbias", [DM], BF16)
    winq = din("winq", [DM, 512]); wink = din("wink", [DM, 512])
    winv = din("winv", [DM, 512]); wing = din("wing", [DM, 512])
    binq = din("binq", [512]); bink = din("bink", [512]); binv = din("binv", [512]); bing = din("bing", [512])
    woutT = din("woutT", [512, DM])
    x2o = nc.dram_tensor("x2o", [DM, L], F32, kind="ExternalOutput").ap()
    pao = nc.dram_tensor("pao", [DM, L], F32, kind="ExternalOutput").ap()
    dbgo = nc.dram_tensor("dbgo", [128, 2*L], F32, kind="ExternalOutput").ap()
    row_sp = nc.dram_tensor("row_sp", [4, L], F32, kind="Internal").ap()
    k2_sp = nc.dram_tensor("k2_sp", [8, L], F32, kind="Internal").ap()
    q_sp = nc.dram_tensor("q_sp", [8*66, L], BF16, kind="Internal").ap()
    k_sp = nc.dram_tensor("k_sp", [8*66, L], BF16, kind="Internal").ap()
    g_sp = nc.dram_tensor("g_sp", [512, L], BF16, kind="Internal").ap()
    h_sp = nc.dram_tensor("h_sp", [512, L], BF16, kind="Internal").ap()

    ctx = ExitStack()
    ARENA = 53120
    arena = ctx.enter_context(nc.sbuf_tensor("arena", [128, ARENA], F32))
    off = [0]
    def af32(n, parts=128, base=0):
        ap = arena[base:base+parts, off[0]:off[0]+n]; off[0] += n
        return ap
    def abf16(n, parts=128, base=0):
        m = (n+1)//2
        return af32(m, parts, base).bitcast(BF16)
    def at(p): off[0] = p
    s_stat = af32(64); s_m1 = af32(32); s_v1 = af32(32); s_Rc = af32(32)
    s_consts = af32(8)
    s_onesb = abf16(2)
    s_projb = af32(8); s_fbc = af32(8); s_fbraw = abf16(16)
    s_scal = af32(8, parts=1)
    s_rsin = af32(1); s_ssk = af32(1); s_rskc = af32(1)
    s_binq = af32(8, parts=64); s_bink = af32(8, parts=64); s_bing = af32(8, parts=64)
    s_m2bq = af32(8, parts=64)
    ROWS = af32(L, parts=128)
    s_q2r = ROWS[0:1, :]
    s_rr = ROWS[32:33, 0:512]; s_rr2 = ROWS[32:33, 512:1024]
    s_lor = ROWS[32:33, 1024:3072].bitcast(BF16)
    s_q2hib = ROWS[32:33, 3072:3328].bitcast(BF16)
    s_onrow = ROWS[64:65, 0:2048].bitcast(BF16)
    s_rrb = abf16(512, parts=1)
    s_onesr = abf16(128, parts=1)
    s_binvr = abf16(512, parts=1)
    PERM = off[0]
    XREG = off[0]
    s_xt = [af32(L) for _ in range(8)]
    XEND = off[0]
    s_comb = [abf16(8*512) for _ in range(2)]
    PFOFF = off[0]
    s_pfb = [af32(512) for _ in range(2)]
    s_psb = [af32(512) for _ in range(2)]
    s_projw = abf16(8*DM)
    s_Rrep = abf16(L); s_mrep = abf16(L)
    s_hnb = [abf16(512) for _ in range(2)]
    s_fo = [abf16(512) for _ in range(2)]
    ST1a = off[0]
    at(PFOFF)
    s_xq = [af32(1024) for _ in range(2)]
    ST1 = max(ST1a, off[0])
    s_sqt = s_comb[0].bitcast(F32)[:, 0:2048].bitcast(BF16)
    s_fq = s_comb[1].bitcast(F32)[:, 0:2048]
    at(XREG)
    s_psi = abf16(8*L)
    s_Vt = abf16(TB*4*520)
    s_qh = [abf16(L, parts=66) for _ in range(2)]
    s_kh = [abf16(L, parts=66) for _ in range(2)]
    s_ght = [abf16(512, parts=64) for _ in range(2)]
    s_wl = [abf16(8*64) for _ in range(6)]
    s_wvb = abf16(8*512)
    QBOFF = off[0]
    s_qb = abf16(L, parts=66)
    s_kb = abf16(L, parts=66)
    s_gb = abf16(L, parts=64)
    s_k2ch = [af32(32) for _ in range(2)]
    s_sq2 = [abf16(512, parts=64) for _ in range(2)]
    AT_REMOVED = 1
    s_sc = [af32(512) for _ in range(2)]
    s_scr = [af32(512) for _ in range(2)]
    s_wq = [abf16(512) for _ in range(2)]
    s_Ant = abf16(512, parts=64)
    s_Hp = [abf16(512) for _ in range(2)]
    s_pot = [af32(512) for _ in range(2)]
    s_At = arena[0:65, off[0]-1024:off[0]-1024+512]
    ST2a = off[0]
    at(QBOFF)
    s_woutw = abf16(4*DM)
    s_hre = [abf16(4*512) for _ in range(2)]
    ST2 = max(ST2a, off[0])
    if dbg_off is not None: dbg_off.update(QBOFF=QBOFF, XREG=XREG, XEND=XEND)
    assert max(ST1, ST2) <= ARENA, (PERM, XEND, ST1, ST2, ARENA)

    def ps_(name, shape):
        return ctx.enter_context(nc.psum_tensor(name, shape, F32))
    p_a = [ps_(f"p_a{i}", [128, 512]) for i in range(2)]
    p_b = [ps_(f"p_b{i}", [128, 512]) for i in range(2)]
    p_st = ps_("p_st", [128, 64])
    p_A = ps_("p_A", [65, 512])
    p_q2 = ps_("p_q2", [1, 512])
    p_rr = ps_("p_rr", [64, 512])

    sems = {}
    def S(name):
        if name not in sems:
            sems[name] = ctx.enter_context(nc.semaphore(name))
        return sems[name]
    ctx.enter_context(nc.allow_non_contiguous_dma(reason="strided param loads"))

    with ctx, nc.Block() as block:
        @block.sync
        def _(s):
            for t in range(8):
                s.dma_start(out=s_xt[t], in_=xb[t*128:(t+1)*128, :]).then_inc(S("ld"), 16)
            s.dma_start(out=s_projb, in_=projb.rearrange("(a b) -> b a", b=128)).then_inc(S("ld"), 16)
            s.dma_start(out=s_fbraw[:, 0:8], in_=fbias.rearrange("(a b) -> b a", b=128)).then_inc(S("ld"), 16)
            s.dma_start(out=s_scal[0:1, 0:1], in_=fsin.unsqueeze(0)).then_inc(S("ld"), 16)
            s.dma_start(out=s_scal[0:1, 1:2], in_=fsk.unsqueeze(0)).then_inc(S("ld"), 16)
            s.dma_start(out=s_binq[0:64, :], in_=binq.rearrange("(a b) -> b a", b=64)).then_inc(S("ld"), 16)
            s.dma_start(out=s_bink[0:64, :], in_=bink.rearrange("(a b) -> b a", b=64)).then_inc(S("ld"), 16)
            s.dma_start(out=s_bing[0:64, :], in_=bing.rearrange("(a b) -> b a", b=64)).then_inc(S("ld"), 16)
        LD0 = 15*16
        @block.gpsimd
        def _(g):
            g.dma_start(out=s_projw.rearrange("p (c m) -> p c m", m=DM),
                        in_=projT.rearrange("(c p) m -> p c m", p=128)).then_inc(S("gw"), 16)
            g.dma_start(out=s_binvr[0:1, :], in_=binv.unsqueeze(0)).then_inc(S("gw"), 16)
            for tb in (range(TB) if (sub is None or sub >= 2) else []):
                for c in range(8):
                    j = tb*8 + c
                    if j >= 2:
                        g.wait_ge(S("vc"), j - 1)
                    g.dma_start(out=s_pfb[j % 2], in_=pf[c*128:(c+1)*128, tb*512:(tb+1)*512]).then_inc(S(f"lpf{j % 2}"), 16)
                    g.dma_start(out=s_psb[j % 2], in_=pso[c*128:(c+1)*128, tb*512:(tb+1)*512]).then_inc(S(f"lps{j % 2}"), 16)
        @block.vector
        def _(v):
            v.memset(s_onesb, 1.0)
            v.memset(s_onesr[0:1, :], 1.0)
            v.memset(s_onrow[0:1, :], 1.0)
            nc.vector.engine_nop().then_inc(S("vor"), 1)
            v.memset(s_consts[:, 0:1], EPS); nc.const_aps.register(F32, EPS, s_consts[:, 0:1])
            v.memset(s_consts[:, 1:2], 1.0); nc.const_aps.register(F32, 1.0, s_consts[:, 1:2])
            v.memset(s_consts[:, 2:3], -2.0); nc.const_aps.register(F32, -2.0, s_consts[:, 2:3])
            v.memset(s_consts[:, 3:4], EPSB); nc.const_aps.register(F32, EPSB, s_consts[:, 3:4])
            v.memset(s_consts[:, 4:5], 0.5); nc.const_aps.register(F32, 0.5, s_consts[:, 4:5])
            v.wait_ge(S("ld"), LD0)
            for tbx in (range(TB + 1) if (sub is None or sub >= 2) else []):
                if tbx < TB:
                    tb = tbx
                    for c in range(8):
                        j = tb*8 + c
                        v.wait_ge(S(f"lpf{j % 2}"), (j//2 + 1)*16)
                        v.wait_ge(S(f"lps{j % 2}"), (j//2 + 1)*16)
                        if tb >= 2 and c == 0 and (sub is None or sub >= 3):
                            v.wait_ge(S("t"), (tb - 1)*8)
                        v.drain()
                        v.tensor_tensor(out=s_comb[tb % 2][:, c*512:(c+1)*512], in0=s_pfb[j % 2],
                                        in1=s_psb[j % 2], op=OP.add).then_inc(S("vc"), 1)
                if tbx >= 1 and (sub is None or sub >= 3):
                    tb = tbx - 1
                    for mt in range(8):
                        j = tb*8 + mt
                        v.wait_ge(S("t"), j + 1)
                        v.drain()
                        v.scalar_tensor_tensor(out=s_xt[mt][:, tb*512:(tb+1)*512], in0=p_a[j % 2][:, :],
                                               scalar=s_projb[:, mt:mt+1], in1=s_xt[mt][:, tb*512:(tb+1)*512],
                                               op0=OP.add, op1=OP.add).then_inc(S("vp"), 1)
        @block.tensor
        def _(te):
            te.wait_ge(S("gw"), 32)
            for tb in (range(TB) if (sub is None or sub >= 3) else []):
                te.wait_ge(S("vc"), (tb + 1)*8)
                for mt in range(8):
                    j = tb*8 + mt
                    if j >= 2:
                        te.wait_ge(S("vp"), j - 1)
                    mm = None
                    for c in range(8):
                        mm = te.matmul(p_a[j % 2][:, :], s_projw[:, c*DM + mt*128:c*DM + (mt+1)*128],
                                       s_comb[tb % 2][:, c*512:(c+1)*512], start=(c == 0), stop=(c == 7))
                    mm.then_inc(S("t"), 1)
        def emit_stats(tag, wait_sem, wait_val, tsem):
            @block.vector
            def _(v):
                for t in range(8):
                    v.wait_ge(S(wait_sem), wait_val)
                    if t >= 1:
                        v.wait_ge(S(tsem), t)
                    v.drain()
                    v.tensor_copy(out=s_sqt, in_=s_xt[t])
                    v.drain()
                    v.tensor_tensor(out=s_Rrep, in0=s_sqt, in1=s_sqt, op=OP.mult).then_inc(S(tag + "v"), 1)
            @block.tensor
            def _(te):
                for t in range(8):
                    te.wait_ge(S(tag + "v"), t + 1)
                    last = None
                    for tt in range(32):
                        te.matmul(p_st[:, 2*tt:2*tt+1], s_sqt[:, tt*128:(tt+1)*128],
                                  s_onesb[0:128, 0:1], start=(t == 0 and tt == 0), stop=False)
                        last = te.matmul(p_st[:, 2*tt+1:2*tt+2], s_Rrep[:, tt*128:(tt+1)*128],
                                         s_onesb[0:128, 0:1], start=False, stop=(t == 7 and tt == 31))
                    last.then_inc(S(tsem), 1)
            @block.vector
            def _(v):
                v.wait_ge(S(tsem), 8)
                v.drain()
                v.tensor_copy(out=s_stat, in_=p_st[:, :])
                v.drain()
                v.tensor_scalar(out=s_m1, in0=s_stat[:, 0:64:2], scalar1=1.0/DM, scalar2=None, op0=OP.mult)
                v.tensor_scalar(out=s_v1, in0=s_stat[:, 1:64:2], scalar1=1.0/DM, scalar2=None, op0=OP.mult)
                v.drain()
                v.tensor_tensor(out=s_Rc, in0=s_m1, in1=s_m1, op=OP.mult)
                v.drain()
                v.tensor_tensor(out=s_v1, in0=s_v1, in1=s_Rc, op=OP.subtract).then_inc(S(tag + "x"), 1)
            @block.scalar
            def _(sc):
                if tag == "sA":
                    sc.preload_act_table(AF.Sqrt)
                sc.wait_ge(S(tag + "x"), 1)
                sc.activation(out=s_Rc, in_=s_v1, func=AF.Sqrt, bias=EPS).then_inc(S(tag + "s"), 1)
            @block.vector
            def _(v):
                v.wait_ge(S(tag + "s"), 1)
                v.drain()
                v.reciprocal(out=s_Rc, in_=s_Rc).then_inc(S(tag + "x"), 1)
            @block.sync
            def _(s):
                s.wait_ge(S(tag + "x"), 2)
                s.dma_start(out=row_sp[0:1, :].rearrange("o (a b) -> (o b) a", b=128), in_=s_Rc).then_inc(S(tag + "r"), 16)
                s.dma_start(out=row_sp[1:2, :].rearrange("o (a b) -> (o b) a", b=128), in_=s_m1).then_inc(S(tag + "r"), 16)
            @block.gpsimd
            def _(g):
                g.wait_ge(S(tag + "r"), 32)
                g.dma_start(out=s_Rrep, in_=row_sp[0:1, :].broadcast_to((128, L))).then_inc(S(tag + "b"), 16)
                g.dma_start(out=s_mrep, in_=row_sp[1:2, :].broadcast_to((128, L))).then_inc(S(tag + "b"), 16)
        if sub is None or sub >= 4:
            emit_stats("sA", "vp", 64, "t2")
        SUB5 = (sub is None or sub >= 5)
        @block.vector
        def _(v):
            if not SUB5:
                return
            v.wait_ge(S("ld"), LD0)
            v.drain()
            v.reciprocal(out=s_scal[0:1, 2:3], in_=s_scal[0:1, 0:1])
            v.reciprocal(out=s_scal[0:1, 3:4], in_=s_scal[0:1, 1:2])
            v.drain()
            v.tensor_tensor(out=s_scal[0:1, 4:5], in0=s_scal[0:1, 0:1], in1=s_scal[0:1, 1:2], op=OP.mult)
            v.drain()
            v.tensor_copy(out=s_fbc, in_=s_fbraw[:, 0:8])
            v.tensor_scalar(out=s_m2bq[0:64, :], in0=s_binq[0:64, :], scalar1=-2.0, scalar2=None, op0=OP.mult).then_inc(S("vsc"), 1)
        @block.sync
        def _(s):
            if not SUB5:
                return
            s.wait_ge(S("vsc"), 1)
            s.dma_start(out=row_sp[2:3, 0:3], in_=s_scal[0:1, 2:5]).then_inc(S("ldw"), 16)
        @block.gpsimd
        def _(g):
            if not SUB5:
                return
            g.wait_ge(S("ldw"), 16)
            g.dma_start(out=s_rsin, in_=row_sp[2:3, 0:1].broadcast_to((128, 1))).then_inc(S("bw"), 16)
            g.dma_start(out=s_rskc, in_=row_sp[2:3, 1:2].broadcast_to((128, 1))).then_inc(S("bw"), 16)
            g.dma_start(out=s_ssk, in_=row_sp[2:3, 2:3].broadcast_to((128, 1))).then_inc(S("bw"), 16)
            g.wait_ge(S("t"), 64)
            g.dma_start(out=s_projw.rearrange("p (c m) -> p c m", m=DM),
                        in_=fker.rearrange("(c p) m -> p c m", p=128)).then_inc(S("gw"), 16)
        @block.vector
        def _(v):
            if not SUB5:
                return
            v.wait_ge(S("bw"), 48)
            v.wait_ge(S("gw"), 48)
            for c in range(8):
                v.tensor_scalar(out=s_fq.bitcast(FP8)[:, c*DM:(c+1)*DM], in0=s_projw[:, c*DM:(c+1)*DM],
                                scalar1=s_rskc[:, 0:1], scalar2=None, op0=OP.mult).then_inc(S("vq"), 1)
        @block.vector
        def _(v):
            if not (sub is None or sub >= 6):
                return
            v.wait_ge(S("sAb"), 32)
            v.wait_ge(S("vq"), 8)
            for tbx in range(TB + 1):
                if tbx < TB:
                    tb = tbx
                    for t in range(8):
                        j = tb*8 + t
                        if tb >= 2 and (sub is None or sub >= 7):
                            v.wait_ge(S("tf"), (tb - 1)*8)
                        v.drain()
                        v.tensor_tensor(out=s_hnb[j % 2], in0=s_xt[t][:, tb*512:(tb+1)*512],
                                        in1=s_mrep[:, tb*512:(tb+1)*512], op=OP.subtract)
                        v.drain()
                        v.tensor_tensor(out=s_hnb[j % 2], in0=s_hnb[j % 2],
                                        in1=s_Rrep[:, tb*512:(tb+1)*512], op=OP.mult)
                        v.drain()
                        v.tensor_scalar(out=s_xq[tb % 2].bitcast(FP8)[:, t*512:(t+1)*512], in0=s_hnb[j % 2],
                                        scalar1=s_rsin[:, 0:1], scalar2=None, op0=OP.mult).then_inc(S("vh"), 1)
                if tbx >= 1 and sub is None:
                    tb = tbx - 1
                    for ft in range(8):
                        j = tb*8 + ft
                        v.wait_ge(S("sf"), j + 1)
                        v.drain()
                        v.tensor_tensor(out=s_xt[ft][:, tb*512:(tb+1)*512], in0=s_xt[ft][:, tb*512:(tb+1)*512],
                                        in1=s_fo[j % 2], op=OP.add).then_inc(S("vf"), 1)
        @block.tensor
        def _(te):
            for tb in (range(TB) if (sub is None or sub >= 7) else []):
                te.wait_ge(S("vh"), (tb + 1)*8)
                for ft in range(8):
                    j = tb*8 + ft
                    if j >= 2:
                        te.wait_ge(S("sf"), j - 1)
                    mm = None
                    for c in range(8):
                        mm = te.matmul(p_b[j % 2][:, :], s_fq.bitcast(FP8)[:, c*DM + ft*128:c*DM + (ft+1)*128],
                                       s_xq[tb % 2].bitcast(FP8)[:, c*512:(c+1)*512], start=(c == 0), stop=(c == 7))
                    mm.then_inc(S("tf"), 1)
        @block.scalar
        def _(sc):
            for tb in (range(TB) if (sub is None or sub >= 7) else []):
                for ft in range(8):
                    j = tb*8 + ft
                    sc.wait_ge(S("tf"), j + 1)
                    if j >= 2 and sub is None:
                        sc.wait_ge(S("vf"), j - 1)
                    sc.activation(out=s_fo[j % 2], in_=p_b[j % 2][:, :], func=AF.Identity,
                                  scale=s_ssk[:, 0:1], bias=s_fbc[:, ft:ft+1]).then_inc(S("sf"), 1)
        if sub is not None and sub <= 7:
            @block.sync
            def _(s):
                s.wait_ge(S("ld"), LD0)
                s.wait_ge(S("gw"), 32)
                if sub >= 2:
                    s.wait_ge(S("vc"), 64)
                if sub >= 3:
                    s.wait_ge(S("vp"), 64)
                if sub >= 4:
                    s.wait_ge(S("sAb"), 32)
                if sub >= 5:
                    s.wait_ge(S("bw"), 48)
                    s.wait_ge(S("vq"), 8)
                if sub >= 6:
                    s.wait_ge(S("vh"), 64)
                if sub >= 7:
                    s.wait_ge(S("sf"), 64)
                for t in range(8):
                    s.dma_start(out=x2o[t*128:(t+1)*128, :], in_=s_xt[t]).then_inc(S("subo"), 16)
                    s.dma_start(out=pao[t*128:(t+1)*128, :], in_=s_xt[t]).then_inc(S("subo"), 16)
                s.dma_start(out=dbgo, in_=arena[:, 0:2*L]).then_inc(S("subo"), 16)
                s.wait_ge(S("subo"), 17*16)
            ctx.close()
            return nc
        @block.sync
        def _(s):
            s.wait_ge(S("vf"), 64)
            for t in range(8):
                s.dma_start(out=x2o[t*128:(t+1)*128, :], in_=s_xt[t]).then_inc(S("sx2"), 16)
        if phases <= 1:
            @block.sync
            def _(s):
                s.wait_ge(S("sx2"), 128)
            ctx.close()
            return nc
        emit_stats("sB", "vf", 64, "t3")
        @block.vector
        def _(v):
            v.wait_ge(S("sBb"), 32)
            for t in range(8):
                v.wait_ge(S("sx2"), 128)
                v.drain()
                v.tensor_tensor(out=s_sqt, in0=s_xt[t], in1=s_mrep, op=OP.subtract)
                v.drain()
                v.tensor_tensor(out=s_psi[:, t*L:(t+1)*L], in0=s_sqt, in1=s_Rrep, op=OP.mult).then_inc(S("vpsi"), 1)
        @block.gpsimd
        def _(g):
            g.wait_ge(S("vpsi"), 8)
            g.dma_start(out=s_wvb.rearrange("p (c m) -> p c m", m=512),
                        in_=winv.rearrange("(c p) m -> p c m", p=128)).then_inc(S("gwv"), 16)
        @block.tensor
        def _(te):
            te.wait_ge(S("gwv"), 16)
            for tt in range(32):
                j = tt
                if j >= 2:
                    te.wait_ge(S("vV"), j - 1)
                tb, half = tt // 4, tt % 4
                mm = None
                for c in range(8):
                    mm = te.matmul(p_a[j % 2][:, :], s_psi[:, c*L + tt*128:c*L + (tt+1)*128],
                                   s_wvb[:, c*512:(c+1)*512], start=(c == 0), stop=False)
                mm = te.matmul(p_a[j % 2][:, :], s_onesb[0:1, 0:1].broadcast_to((1, 128)),
                               s_binvr[0:1, :], start=False, stop=True)
                mm.then_inc(S("tV"), 1)
        @block.vector
        def _(v):
            for tt in range(32):
                v.wait_ge(S("tV"), tt + 1)
                v.drain()
                v.tensor_copy(out=s_Vt[:, tt*520:(tt+1)*520].rearrange("p (h e) -> p h e", e=65)[:, :, 0:64],
                              in_=p_a[tt % 2][:, :].rearrange("p (h e) -> p h e", e=64))
                if tt == 0:
                    pass
                v.memset(s_Vt[:, tt*520:(tt+1)*520].rearrange("p (h e) -> p h e", e=65)[:, :, 64:65], 1.0)
                v.drain()
                nc.vector.engine_nop().then_inc(S("vV"), 1)
        @block.tensor
        def _(te):
            te.wait_ge(S("vV"), 32)
            te.wait_ge(S("sf"), 64)
            for h in range(8):
                te.wait_ge(S(f"lwq{h % 2}"), (h//2 + 1)*16)
                for tb in range(TB):
                    if h > 0 or tb >= 2:
                        te.wait_ge(S("scq"), h*8 + tb - 2 + 1)
                    mm = None
                    for c in range(8):
                        mm = te.matmul(p_a[tb % 2][0:64, :], s_wl[h % 2][:, c*64:(c+1)*64],
                                       s_psi[:, c*L + tb*512:c*L + (tb+1)*512], start=(c == 0), stop=(c == 7))
                    mm.then_inc(S("teq"), 1)
                te.wait_ge(S(f"lwk{h % 2}"), (h//2 + 1)*16)
                for tb in range(TB):
                    if h > 0 or tb >= 2:
                        te.wait_ge(S("sck"), h*8 + tb - 2 + 1)
                    mm = None
                    for c in range(8):
                        mm = te.matmul(p_b[tb % 2][0:64, :], s_wl[2 + h % 2][:, c*64:(c+1)*64],
                                       s_psi[:, c*L + tb*512:c*L + (tb+1)*512], start=(c == 0), stop=(c == 7))
                    mm.then_inc(S("tek"), 1)
                for tb in range(TB):
                    te.wait_ge(S("vSq"), h*16 + tb + 1)
                    if h*16 + tb >= 1:
                        te.wait_ge(S("vq2"), h*16 + tb)
                    te.matmul(p_q2[0:1, :], s_onesb[0:64, 0:1], s_sq2[tb % 2][0:64, :],
                              start=True, stop=True).then_inc(S("tq2"), 1)
                for tb in range(TB):
                    te.wait_ge(S("vSq"), h*16 + 8 + tb + 1)
                    te.wait_ge(S("vq2"), h*16 + 8 + tb)
                    te.matmul(p_q2[0:1, :], s_onesb[0:64, 0:1], s_sq2[tb % 2][0:64, :],
                              start=True, stop=True).then_inc(S("tq2"), 1)
                te.wait_ge(S(f"lwg{h % 2}"), (h//2 + 1)*16)
                for tb in range(TB):
                    if h > 0 or tb >= 2:
                        te.wait_ge(S("scg"), h*8 + tb - 2 + 1)
                    mm = None
                    for c in range(8):
                        mm = te.matmul(p_st[0:64, 0:64] if False else p_a[tb % 2][64:128, :],
                                       s_wl[4 + h % 2][:, c*64:(c+1)*64],
                                       s_psi[:, c*L + tb*512:c*L + (tb+1)*512], start=(c == 0), stop=(c == 7))
                    mm.then_inc(S("teg"), 1)
        @block.scalar
        def _(sc):
            for h in range(8):
                for tb in range(TB):
                    sc.wait_ge(S("teq"), h*8 + tb + 1)
                    if h > 0:
                        sc.wait_ge(S("spq"), h*32)
                    sc.activation(out=s_qb[0:64, tb*512:(tb+1)*512], in_=p_a[tb % 2][0:64, :],
                                  func=AF.Identity, scale=-2.0, bias=s_m2bq[:, h:h+1]).then_inc(S("scq"), 1)
                for tb in range(TB):
                    sc.wait_ge(S("tek"), h*8 + tb + 1)
                    if h > 0:
                        sc.wait_ge(S("spk"), h*16)
                    sc.activation(out=s_kb[0:64, tb*512:(tb+1)*512], in_=p_b[tb % 2][0:64, :],
                                  func=AF.Identity, bias=s_bink[:, h:h+1]).then_inc(S("sck"), 1)
                for tb in range(TB):
                    sc.wait_ge(S("teg"), h*8 + tb + 1)
                    if h > 0:
                        sc.wait_ge(S("spg"), h*16)
                    sc.activation(out=s_gb[0:64, tb*512:(tb+1)*512], in_=p_a[tb % 2][64:128, :],
                                  func=AF.Identity, bias=s_bing[:, h:h+1]).then_inc(S("scg"), 1)
        @block.vector
        def _(v):
            for h in range(8):
                for tb in range(TB):
                    v.wait_ge(S("scq"), h*8 + tb + 1)
                    if h >= 1 and tb == 0:
                        v.wait_ge(S("spq"), h*32)
                    if h*16 + tb >= 2:
                        v.wait_ge(S("tq2"), h*16 + tb - 1)
                    v.drain()
                    v.tensor_tensor(out=s_sq2[tb % 2][0:64, :], in0=s_qb[0:64, tb*512:(tb+1)*512],
                                    in1=s_qb[0:64, tb*512:(tb+1)*512], op=OP.mult).then_inc(S("vSq"), 1)
                    v.wait_ge(S("tq2"), h*16 + tb + 1)
                    v.drain()
                    v.tensor_scalar(out=s_q2hib, in0=p_q2[0:1, :],
                                    scalar1=0.25, scalar2=None, op0=OP.mult)
                    v.drain()
                    v.tensor_copy(out=s_qb[64:65, tb*512:(tb+1)*512], in_=s_q2hib)
                    v.scalar_tensor_tensor(out=s_lor[0:1, tb*512:(tb+1)*512], in0=p_q2[0:1, :],
                                           scalar=0.25, in1=s_q2hib,
                                           op0=OP.mult, op1=OP.subtract).then_inc(S("vq2"), 1)
                v.drain()
                nc.vector.engine_nop().then_inc(S("vrq"), 1)
                for tb in range(TB):
                    v.wait_ge(S("sck"), h*8 + tb + 1)
                    if h >= 1 and tb == 0:
                        v.wait_ge(S("sk2"), h*16)
                    v.wait_ge(S("tq2"), h*16 + 8 + tb - 1) if h*16 + 8 + tb >= 2 else None
                    v.drain()
                    v.tensor_tensor(out=s_sq2[tb % 2][0:64, :], in0=s_kb[0:64, tb*512:(tb+1)*512],
                                    in1=s_kb[0:64, tb*512:(tb+1)*512], op=OP.mult).then_inc(S("vSq"), 1)
                    v.wait_ge(S("tq2"), h*16 + 8 + tb + 1)
                    v.drain()
                    v.tensor_copy(out=s_q2r[0:1, tb*512:(tb+1)*512], in_=p_q2[0:1, :]).then_inc(S("vq2"), 1)
                v.drain()
                nc.vector.engine_nop().then_inc(S("vrk"), 1)
        @block.gpsimd
        def _(g):
            g.wait_ge(S("vpsi"), 8)
            g.wait_ge(S("vor"), 1)
            for h in range(8):
                g.dma_start(out=k_sp[h*66+64:h*66+65, :], in_=s_onrow[0:1, :]).then_inc(S("spk1"), 16)
                g.dma_start(out=k_sp[h*66+65:h*66+66, :], in_=s_onrow[0:1, :]).then_inc(S("spk1"), 16)
            for h in range(9):
                if h < 8:
                    if h >= 2:
                        g.wait_ge(S("teq"), (h - 1)*8)
                    g.dma_start(out=s_wl[h % 2].rearrange("p (c m) -> p c m", m=64),
                                in_=winq[:, h*64:(h+1)*64].rearrange("(c p) m -> p c m", p=128)).then_inc(S(f"lwq{h % 2}"), 16)
                    if h >= 2:
                        g.wait_ge(S("tek"), (h - 1)*8)
                    g.dma_start(out=s_wl[2 + h % 2].rearrange("p (c m) -> p c m", m=64),
                                in_=wink[:, h*64:(h+1)*64].rearrange("(c p) m -> p c m", p=128)).then_inc(S(f"lwk{h % 2}"), 16)
                    if h >= 2:
                        g.wait_ge(S("teg"), (h - 1)*8)
                    g.dma_start(out=s_wl[4 + h % 2].rearrange("p (c m) -> p c m", m=64),
                                in_=wing[:, h*64:(h+1)*64].rearrange("(c p) m -> p c m", p=128)).then_inc(S(f"lwg{h % 2}"), 16)
                if h >= 1:
                    hp = h - 1
                    g.wait_ge(S("vrq"), hp + 1)
                    g.dma_start(out=q_sp[hp*66:hp*66+65, :], in_=s_qb[0:65, :]).then_inc(S("spq"), 16)
                    g.dma_start(out=q_sp[hp*66+65:hp*66+66, :], in_=s_lor[0:1, :]).then_inc(S("spq"), 16)
                    g.wait_ge(S("vrk"), hp + 1)
                    g.dma_start(out=k2_sp[hp:hp+1, :], in_=s_q2r[0:1, :]).then_inc(S("sk2"), 16)
                    g.dma_start(out=k_sp[hp*66:hp*66+64, :], in_=s_kb[0:64, :]).then_inc(S("spk"), 16)
                    g.wait_ge(S("scg"), hp*8 + 8)
                    g.dma_start(out=g_sp[hp*64:(hp+1)*64, :], in_=s_gb[0:64, :]).then_inc(S("spg"), 16)
        if phases <= 2:
            @block.sync
            def _(s):
                s.wait_ge(S("spq"), 8*16); s.wait_ge(S("spk"), 8*16); s.wait_ge(S("spg"), 8*16)
                s.wait_ge(S("vV"), 32)
            ctx.close()
            return nc
        @block.gpsimd
        def _(g):
            g.wait_ge(S("spk1"), 16*16)
            for h in range(8):
                g.wait_ge(S("spq"), (h + 1)*32)
                g.wait_ge(S("spk"), (h + 1)*16)
                g.wait_ge(S("spg"), (h + 1)*16)
                g.wait_ge(S("sk2"), (h + 1)*16)
                if h >= 2:
                    g.wait_ge(S("tA"), (h - 1)*8)
                g.dma_start(out=s_qh[h % 2][0:66, :], in_=q_sp[h*66:(h+1)*66, :]).then_inc(S(f"rq{h % 2}"), 16)
                g.dma_start(out=s_kh[h % 2][0:66, :], in_=k_sp[h*66:(h+1)*66, :]).then_inc(S(f"rk{h % 2}"), 16)
                g.dma_start(out=s_k2ch[h % 2], in_=k2_sp[h:h+1, :].rearrange("o (a b) -> (o b) a", b=128)).then_inc(S(f"rc{h % 2}"), 16)
                for qb in range(TB):
                    j = h*8 + qb
                    if j >= 2:
                        g.wait_ge(S("vH"), j - 1)
                    g.dma_start(out=s_ght[j % 2][0:64, :], in_=g_sp[h*64:(h+1)*64, qb*512:(qb+1)*512]).then_inc(S(f"rg{j % 2}"), 16)
                    if j >= 2:
                        jp = j - 2
                        hp, qp = jp // 8, jp % 8
                        g.dma_start(out=h_sp[hp*64:(hp+1)*64, qp*512:(qp+1)*512],
                                    in_=s_Hp[jp % 2][0:64, :]).then_inc(S(f"sph{jp % 2}"), 16)
                if h == 7:
                    for jp in range(62, 64):
                        hp, qp = jp // 8, jp % 8
                        g.wait_ge(S("vH"), jp + 1)
                        g.dma_start(out=h_sp[hp*64:(hp+1)*64, qp*512:(qp+1)*512],
                                    in_=s_Hp[jp % 2][0:64, :]).then_inc(S(f"sph{jp % 2}"), 16)
        @block.tensor
        def _(te):
            for h in range(8):
                te.wait_ge(S(f"rq{h % 2}"), (h//2 + 1)*16)
                te.wait_ge(S(f"rk{h % 2}"), (h//2 + 1)*16)
                for qb in range(TB):
                    j = h*8 + qb
                    for ct in range(32):
                        jc = j*32 + ct
                        if jc >= 2:
                            te.wait_ge(S("ssq"), jc - 1)
                        te.matmul(p_b[ct % 2][:, :], s_kh[h % 2][0:66, ct*128:(ct+1)*128],
                                  s_qh[h % 2][0:66, qb*512:(qb+1)*512], start=True, stop=True).then_inc(S("tsc"), 1)
                        te.wait_ge(S("vW"), jc + 1 - 2 if jc >= 2 else 0) if False else None
                        if True:
                            te.wait_ge(S("vW"), jc + 1)
                            te.matmul(p_A[:, :], s_Vt[:, ct*520 + (h % 8)*0:ct*520 + 520][:, h*65:(h+1)*65] if False else s_Vt[:, ct*520 + h*65: ct*520 + h*65 + 65],
                                      s_wq[ct % 2][:, :], start=(ct == 0), stop=(ct == 31)).then_inc(S("tA2"), 1)
                    te.wait_ge(S("vA"), j + 1)
                    if j >= 1:
                        te.wait_ge(S("vH"), j)
                    te.matmul(p_rr[:, :], s_onesr[0:1, 0:64], s_rrb[0:1, :], start=True, stop=True).then_inc(S("tR"), 1)
                    nc.tensor.nop().then_inc(S("tA"), 1)
        @block.scalar
        def _(sc):
            _load_table(nc, "natural_log_exp_and_others")
            for h in range(8):
                sc.wait_ge(S(f"rc{h % 2}"), (h//2 + 1)*16)
                for qb in range(TB):
                    for ct in range(32):
                        jc = (h*8 + qb)*32 + ct
                        sc.wait_ge(S("tsc"), jc + 1)
                        sc.activation(out=s_sc[ct % 2], in_=p_b[ct % 2][:, :], func=AF.Ln,
                                      bias=s_k2ch[h % 2][:, ct:ct+1]).then_inc(S("ssq"), 1)
                        sc.drain()
                        sc.activation(out=s_scr[ct % 2], in_=s_sc[ct % 2], func=AF.Exp, scale=0.5)
                        sc.drain()
                        sc.activation(out=s_sc[ct % 2], in_=s_scr[ct % 2], func=AF.Ln, bias=EPSB)
                        sc.drain()
                        sc.activation(out=s_wq[ct % 2], in_=s_sc[ct % 2], func=AF.Exp,
                                      scale=-2.0).then_inc(S("vW"), 1)
        @block.vector
        def _(v):
            for h in range(8):
                for qb in range(TB):
                    j = h*8 + qb
                    v.wait_ge(S("tA2"), j*32 + 32)
                    v.drain()
                    v.tensor_copy(out=s_At[0:65, :], in_=p_A[:, :])
                    v.drain()
                    v.tensor_scalar(out=s_rr[0:1, :], in0=s_At[64:65, :], scalar1=EPSB, scalar2=None, op0=OP.add)
                    v.drain()
                    v.reciprocal(out=s_rr2[0:1, :], in_=s_rr[0:1, :])
                    v.drain()
                    v.tensor_copy(out=s_rrb[0:1, :], in_=s_rr2[0:1, :]).then_inc(S("vA"), 1)
                    v.wait_ge(S("tR"), j + 1)
                    v.wait_ge(S(f"rg{j % 2}"), (j//2 + 1)*16)
                    if j >= 2:
                        v.wait_ge(S(f"sph{j % 2}"), (j//2)*16)
                    v.drain()
                    v.tensor_tensor(out=s_Ant[0:64, :], in0=s_At[0:64, :], in1=p_rr[:, :], op=OP.mult)
                    v.drain()
                    v.tensor_tensor(out=s_Hp[j % 2][0:64, :], in0=s_Ant[0:64, :],
                                    in1=s_ght[j % 2][0:64, :], op=OP.mult).then_inc(S("vH"), 1)
        if phases <= 3:
            @block.sync
            def _(s):
                for sl in range(2):
                    s.wait_ge(S(f"sph{sl}"), 32*16)
            ctx.close()
            return nc
        @block.gpsimd
        def _(g):
            for sl in range(2):
                g.wait_ge(S(f"sph{sl}"), 32*16)
            g.dma_start(out=s_woutw.rearrange("p (c m) -> p c m", m=DM),
                        in_=woutT.rearrange("(c p) m -> p c m", p=128)).then_inc(S("gww"), 16)
            for tb in range(TB):
                if tb >= 2:
                    g.wait_ge(S("tW"), (tb - 1)*8)
                for c in range(4):
                    g.dma_start(out=s_hre[tb % 2][:, c*512:(c+1)*512],
                                in_=h_sp[c*128:(c+1)*128, tb*512:(tb+1)*512]).then_inc(S(f"rh{tb % 2}"), 16)
        @block.tensor
        def _(te):
            te.wait_ge(S("gww"), 16)
            for tb in range(TB):
                te.wait_ge(S(f"rh{tb % 2}"), (tb//2 + 1)*4*16)
                for mt in range(8):
                    j = tb*8 + mt
                    if j >= 2:
                        te.wait_ge(S("vWo"), j - 1)
                    mm = None
                    for c in range(4):
                        mm = te.matmul(p_a[j % 2][:, :], s_woutw[:, c*DM + mt*128:c*DM + (mt+1)*128],
                                       s_hre[tb % 2][:, c*512:(c+1)*512], start=(c == 0), stop=(c == 3))
                    mm.then_inc(S("tW"), 1)
        @block.vector
        def _(v):
            for tb in range(TB):
                for mt in range(8):
                    j = tb*8 + mt
                    v.wait_ge(S("tW"), j + 1)
                    if j >= 2:
                        v.wait_ge(S(f"spo{j % 2}"), (j//2)*16)
                    v.drain()
                    v.tensor_copy(out=s_pot[j % 2], in_=p_a[j % 2][:, :]).then_inc(S("vWo"), 1)
        @block.sync
        def _(s):
            for tb in range(TB):
                for mt in range(8):
                    j = tb*8 + mt
                    s.wait_ge(S("vWo"), j + 1)
                    s.dma_start(out=pao[mt*128:(mt+1)*128, tb*512:(tb+1)*512],
                                in_=s_pot[j % 2]).then_inc(S(f"spo{j % 2}"), 16)
            s.wait_ge(S("spo0"), 32*16); s.wait_ge(S("spo1"), 32*16)
            s.wait_ge(S("sx2"), 128)
        ctx.close()
        return nc


def build_l3():
    nc = bass.Bass("TRN2", target_bir_lowering=False)
    def din(name, shape, dt=F32):
        return nc.dram_tensor(name, shape, dt, kind="ExternalInput").ap()
    x2i = din("x2i", [DM, L]); pA = din("pA", [DM, L]); pB = din("pB", [DM, L])
    bout = din("bout", [DM])
    fker = din("fker", [DM, DM], BF16); fsin = din("fsin", [1]); fsk = din("fsk", [1])
    fbias = din("fbias", [DM], BF16)
    x4o = nc.dram_tensor("x4o", [DM, L], F32, kind="ExternalOutput").ap()
    row_sp = nc.dram_tensor("row_sp", [4, L], F32, kind="Internal").ap()

    ctx = ExitStack()
    ARENA = 53120
    arena = ctx.enter_context(nc.sbuf_tensor("arena", [128, ARENA], F32))
    off = [0]
    def af32(n, parts=128):
        ap = arena[0:parts, off[0]:off[0]+n]; off[0] += n
        return ap
    def abf16(n, parts=128):
        m = (n+1)//2
        return af32(m, parts).bitcast(BF16)
    s_stat = af32(64); s_m1 = af32(32); s_v1 = af32(32); s_Rc = af32(32)
    s_consts = af32(8)
    s_onesb = abf16(2)
    s_bout = af32(8); s_fbc = af32(8); s_fbraw = abf16(16)
    s_scal = af32(8, parts=1)
    s_rsin = af32(1); s_ssk = af32(1); s_rskc = af32(1)
    s_xt = [af32(L) for _ in range(8)]
    s_pa = [af32(512) for _ in range(2)]
    s_pb = [af32(512) for _ in range(2)]
    s_fkw = abf16(8*DM)
    s_fq = af32(2048)
    s_Rrep = abf16(L); s_mrep = abf16(L)
    s_sqt = abf16(L)
    s_hnb = [abf16(512) for _ in range(2)]
    s_xq = [af32(1024) for _ in range(2)]
    s_fo = [abf16(512) for _ in range(2)]
    assert off[0] <= ARENA, off[0]

    def ps_(name, shape):
        return ctx.enter_context(nc.psum_tensor(name, shape, F32))
    p_b = [ps_(f"p_b{i}", [128, 512]) for i in range(2)]
    p_st = ps_("p_st", [128, 64])
    sems = {}
    def S(name):
        if name not in sems:
            sems[name] = ctx.enter_context(nc.semaphore(name))
        return sems[name]
    ctx.enter_context(nc.allow_non_contiguous_dma(reason="strided param loads"))

    with ctx, nc.Block() as block:
        @block.sync
        def _(s):
            for t in range(8):
                s.dma_start(out=s_xt[t], in_=x2i[t*128:(t+1)*128, :]).then_inc(S("ld"), 16)
            s.dma_start(out=s_bout, in_=bout.rearrange("(a b) -> b a", b=128)).then_inc(S("ld"), 16)
            s.dma_start(out=s_fbraw[:, 0:8], in_=fbias.rearrange("(a b) -> b a", b=128)).then_inc(S("ld"), 16)
            s.dma_start(out=s_scal[0:1, 0:1], in_=fsin.unsqueeze(0)).then_inc(S("ld"), 16)
            s.dma_start(out=s_scal[0:1, 1:2], in_=fsk.unsqueeze(0)).then_inc(S("ld"), 16)
        LD0 = 12*16
        @block.gpsimd
        def _(g):
            g.dma_start(out=s_fkw.rearrange("p (c m) -> p c m", m=DM),
                        in_=fker.rearrange("(c p) m -> p c m", p=128)).then_inc(S("gw"), 16)
            for t in range(8):
                for tb in range(TB):
                    j = t*TB + tb
                    if j >= 2:
                        g.wait_ge(S("vx3"), j - 1)
                    g.dma_start(out=s_pa[j % 2], in_=pA[t*128:(t+1)*128, tb*512:(tb+1)*512]).then_inc(S(f"lpa{j % 2}"), 16)
                    g.dma_start(out=s_pb[j % 2], in_=pB[t*128:(t+1)*128, tb*512:(tb+1)*512]).then_inc(S(f"lpb{j % 2}"), 16)
        @block.vector
        def _(v):
            v.memset(s_onesb, 1.0)
            v.memset(s_consts[:, 0:1], EPS); nc.const_aps.register(F32, EPS, s_consts[:, 0:1])
            v.memset(s_consts[:, 1:2], 1.0); nc.const_aps.register(F32, 1.0, s_consts[:, 1:2])
            v.wait_ge(S("ld"), LD0)
            for t in range(8):
                for tb in range(TB):
                    j = t*TB + tb
                    v.wait_ge(S(f"lpa{j % 2}"), (j//2 + 1)*16)
                    v.wait_ge(S(f"lpb{j % 2}"), (j//2 + 1)*16)
                    v.drain()
                    v.tensor_tensor(out=s_pa[j % 2], in0=s_pa[j % 2], in1=s_pb[j % 2], op=OP.add)
                    v.drain()
                    v.scalar_tensor_tensor(out=s_xt[t][:, tb*512:(tb+1)*512], in0=s_pa[j % 2],
                                           scalar=s_bout[:, t:t+1], in1=s_xt[t][:, tb*512:(tb+1)*512],
                                           op0=OP.add, op1=OP.add).then_inc(S("vx3"), 1)
        @block.vector
        def _(v):
            for t in range(8):
                v.wait_ge(S("vx3"), 64)
                if t >= 1:
                    v.wait_ge(S("t2"), t)
                v.drain()
                v.tensor_copy(out=s_sqt, in_=s_xt[t])
                v.drain()
                v.tensor_tensor(out=s_Rrep, in0=s_sqt, in1=s_sqt, op=OP.mult).then_inc(S("vs1"), 1)
        @block.tensor
        def _(te):
            for t in range(8):
                te.wait_ge(S("vs1"), t + 1)
                last = None
                for tt in range(32):
                    te.matmul(p_st[:, 2*tt:2*tt+1], s_sqt[:, tt*128:(tt+1)*128],
                              s_onesb[0:128, 0:1], start=(t == 0 and tt == 0), stop=False)
                    last = te.matmul(p_st[:, 2*tt+1:2*tt+2], s_Rrep[:, tt*128:(tt+1)*128],
                                     s_onesb[0:128, 0:1], start=False, stop=(t == 7 and tt == 31))
                last.then_inc(S("t2"), 1)
        @block.vector
        def _(v):
            v.wait_ge(S("t2"), 8)
            v.drain()
            v.tensor_copy(out=s_stat, in_=p_st[:, :])
            v.drain()
            v.tensor_scalar(out=s_m1, in0=s_stat[:, 0:64:2], scalar1=1.0/DM, scalar2=None, op0=OP.mult)
            v.tensor_scalar(out=s_v1, in0=s_stat[:, 1:64:2], scalar1=1.0/DM, scalar2=None, op0=OP.mult)
            v.drain()
            v.tensor_tensor(out=s_Rc, in0=s_m1, in1=s_m1, op=OP.mult)
            v.drain()
            v.tensor_tensor(out=s_v1, in0=s_v1, in1=s_Rc, op=OP.subtract).then_inc(S("v2"), 1)
        @block.scalar
        def _(sc):
            sc.preload_act_table(AF.Sqrt)
            sc.wait_ge(S("v2"), 1)
            sc.activation(out=s_Rc, in_=s_v1, func=AF.Sqrt, bias=EPS).then_inc(S("s3"), 1)
        @block.vector
        def _(v):
            v.wait_ge(S("s3"), 1)
            v.drain()
            v.reciprocal(out=s_Rc, in_=s_Rc)
            v.wait_ge(S("ld"), LD0)
            v.drain()
            v.reciprocal(out=s_scal[0:1, 2:3], in_=s_scal[0:1, 0:1])
            v.reciprocal(out=s_scal[0:1, 3:4], in_=s_scal[0:1, 1:2])
            v.drain()
            v.tensor_tensor(out=s_scal[0:1, 4:5], in0=s_scal[0:1, 0:1], in1=s_scal[0:1, 1:2], op=OP.mult)
            v.drain()
            v.tensor_copy(out=s_fbc, in_=s_fbraw[:, 0:8]).then_inc(S("v2"), 1)
        @block.sync
        def _(s):
            s.wait_ge(S("v2"), 2)
            s.dma_start(out=row_sp[0:1, :].rearrange("o (a b) -> (o b) a", b=128), in_=s_Rc).then_inc(S("ldr"), 16)
            s.dma_start(out=row_sp[1:2, :].rearrange("o (a b) -> (o b) a", b=128), in_=s_m1).then_inc(S("ldr"), 16)
            s.dma_start(out=row_sp[2:3, 0:3], in_=s_scal[0:1, 2:5]).then_inc(S("ldw"), 16)
        @block.gpsimd
        def _(g):
            g.wait_ge(S("ldr"), 32)
            g.dma_start(out=s_Rrep, in_=row_sp[0:1, :].broadcast_to((128, L))).then_inc(S("bc"), 16)
            g.dma_start(out=s_mrep, in_=row_sp[1:2, :].broadcast_to((128, L))).then_inc(S("bc"), 16)
            g.wait_ge(S("ldw"), 16)
            g.dma_start(out=s_rsin, in_=row_sp[2:3, 0:1].broadcast_to((128, 1))).then_inc(S("bw"), 16)
            g.dma_start(out=s_rskc, in_=row_sp[2:3, 1:2].broadcast_to((128, 1))).then_inc(S("bw"), 16)
            g.dma_start(out=s_ssk, in_=row_sp[2:3, 2:3].broadcast_to((128, 1))).then_inc(S("bw"), 16)
        @block.vector
        def _(v):
            v.wait_ge(S("bw"), 48)
            v.wait_ge(S("gw"), 16)
            for c in range(8):
                v.drain()
                v.tensor_scalar(out=s_fq.bitcast(FP8)[:, c*DM:(c+1)*DM], in0=s_fkw[:, c*DM:(c+1)*DM],
                                scalar1=s_rskc[:, 0:1], scalar2=None, op0=OP.mult).then_inc(S("vq"), 1)
        @block.vector
        def _(v):
            v.wait_ge(S("bc"), 32)
            v.wait_ge(S("vq"), 8)
            for tbx in range(TB + 1):
                if tbx < TB:
                    tb = tbx
                    for t in range(8):
                        j = tb*8 + t
                        if tb >= 2:
                            v.wait_ge(S("tf"), (tb - 1)*8)
                        v.drain()
                        v.tensor_tensor(out=s_hnb[j % 2], in0=s_xt[t][:, tb*512:(tb+1)*512],
                                        in1=s_mrep[:, tb*512:(tb+1)*512], op=OP.subtract)
                        v.drain()
                        v.tensor_tensor(out=s_hnb[j % 2], in0=s_hnb[j % 2],
                                        in1=s_Rrep[:, tb*512:(tb+1)*512], op=OP.mult)
                        v.drain()
                        v.tensor_scalar(out=s_xq[tb % 2].bitcast(FP8)[:, t*512:(t+1)*512], in0=s_hnb[j % 2],
                                        scalar1=s_rsin[:, 0:1], scalar2=None, op0=OP.mult).then_inc(S("vh"), 1)
                if tbx >= 1:
                    tb = tbx - 1
                    for ft in range(8):
                        j = tb*8 + ft
                        v.wait_ge(S("sf"), j + 1)
                        if j >= 2:
                            v.wait_ge(S(f"so{j % 2}"), (j//2)*16)
                        v.drain()
                        v.tensor_tensor(out=s_xt[ft][:, tb*512:(tb+1)*512], in0=s_xt[ft][:, tb*512:(tb+1)*512],
                                        in1=s_fo[j % 2], op=OP.add).then_inc(S("vf"), 1)
        @block.tensor
        def _(te):
            for tb in range(TB):
                te.wait_ge(S("vh"), (tb + 1)*8)
                for ft in range(8):
                    j = tb*8 + ft
                    if j >= 2:
                        te.wait_ge(S("sf"), j - 1)
                    mm = None
                    for c in range(8):
                        mm = te.matmul(p_b[j % 2][:, :], s_fq.bitcast(FP8)[:, c*DM + ft*128:c*DM + (ft+1)*128],
                                       s_xq[tb % 2].bitcast(FP8)[:, c*512:(c+1)*512], start=(c == 0), stop=(c == 7))
                    mm.then_inc(S("tf"), 1)
        @block.scalar
        def _(sc):
            for tb in range(TB):
                for ft in range(8):
                    j = tb*8 + ft
                    sc.wait_ge(S("tf"), j + 1)
                    if j >= 2:
                        sc.wait_ge(S("vf"), j - 1)
                    sc.activation(out=s_fo[j % 2], in_=p_b[j % 2][:, :], func=AF.Identity,
                                  scale=s_ssk[:, 0:1], bias=s_fbc[:, ft:ft+1]).then_inc(S("sf"), 1)
        @block.sync
        def _(s):
            for tb in range(TB):
                for ft in range(8):
                    j = tb*8 + ft
                    s.wait_ge(S("vf"), j + 1)
                    s.dma_start(out=x4o[ft*128:(ft+1)*128, tb*512:(tb+1)*512],
                                in_=s_xt[ft][:, tb*512:(tb+1)*512]).then_inc(S(f"so{j % 2}"), 16)
            s.wait_ge(S("so0"), 32*16); s.wait_ge(S("so1"), 32*16)
    return nc



def _l1_inputs(core, ii):
    b, tr = core // 2, core % 2
    pre = "fast" if tr == 0 else "slow"
    gg = lambda k: ii[f"L0_attn_a_{pre}_{k}"]
    tw = ii["L0_attn_a_track_weights"]
    return dict(
        xb=np.ascontiguousarray(ii["L0_attn_a_hidden_states"][b].T),
        normw=ii["L0_attn_a_norm_w"], normb=ii["L0_attn_a_norm_b"],
        inwT=np.ascontiguousarray(gg("in_w").T),
        convw=gg("conv_w"), convb=gg("conv_b"),
        xwT=np.ascontiguousarray(gg("x_w").T),
        dtpwT=np.ascontiguousarray(gg("dtp_w").T),
        dtpb=gg("dtp_b"), alog=gg("A_log"), dvec=gg("D"),
        moutwT=np.ascontiguousarray(gg("mout_w").T),
        twl=(tw if tr == 0 else tw[::-1]).copy(),
    )

def _l2_inputs(core, ii, parts):
    b, h = core // 2, core % 2
    W_in = ii["L0_attn_b_W_in"]
    cols = np.arange(h*8, h*8+8)
    def colsl(off):
        idx = np.concatenate([np.arange(256*j+off, 256*j+off+64) for j in cols])
        return np.ascontiguousarray(W_in[:, idx]), idx
    wq, iq = colsl(0); wk, ik = colsl(64); wv, iv = colsl(128); wg, ig = colsl(192)
    b_in = ii["L0_attn_b_b_in"]
    wout_rows = np.concatenate([np.arange(64*j, 64*j+64) for j in cols])
    return dict(
        xb=np.ascontiguousarray(ii["L0_attn_a_hidden_states"][b].T),
        pf=parts[2*b], pso=parts[2*b+1],
        projT=np.ascontiguousarray(ii["L0_attn_a_proj_w"].T),
        projb=ii["L0_attn_a_proj_b"],
        fker=ii["L0_ffn_kernel"], fsin=ii["L0_ffn_input_scale"], fsk=ii["L0_ffn_kernel_scale"],
        fbias=ii["L0_ffn_bias"],
        winq=wq, wink=wk, winv=wv, wing=wg,
        binq=np.ascontiguousarray(b_in[iq]), bink=np.ascontiguousarray(b_in[ik]),
        binv=np.ascontiguousarray(b_in[iv]), bing=np.ascontiguousarray(b_in[ig]),
        woutT=np.ascontiguousarray(ii["L0_attn_b_W_out"][wout_rows, :]),
    )

def _l3_inputs(core, ii, x2s, paos):
    b = core // 2
    return dict(
        x2i=x2s[2*b], pA=paos[2*b], pB=paos[2*b+1],
        bout=ii["L0_attn_b_b_out"],
        fker=ii["L0_ffn_kernel"], fsin=ii["L0_ffn_input_scale"], fsk=ii["L0_ffn_kernel_scale"],
        fbias=ii["L0_ffn_bias"],
    )

def kernel(**inputs):
    ii = inputs
    nc1 = build_l1()
    res1 = run_bass_kernel_spmd(nc1, [_l1_inputs(c, ii) for c in range(8)], core_ids=list(range(8)))
    parts = [np.ascontiguousarray(res1.results[c]["part"]) for c in range(8)]
    nc2 = build_l2()
    res2 = run_bass_kernel_spmd(nc2, [_l2_inputs(c, ii, parts) for c in range(8)], core_ids=list(range(8)))
    x2s = [np.ascontiguousarray(res2.results[c]["x2o"]) for c in range(8)]
    paos = [np.ascontiguousarray(res2.results[c]["pao"]) for c in range(8)]
    nc3 = build_l3()
    res3 = run_bass_kernel_spmd(nc3, [_l3_inputs(c, ii, x2s, paos) for c in range(8)], core_ids=list(range(8)))
    out = np.stack([res3.results[2*b]["x4o"].T for b in range(4)])
    return np.ascontiguousarray(out.astype(np.float32))
